# version 09
# baseline (speedup 1.0000x reference)
import numpy as np
import ml_dtypes
from contextlib import ExitStack

import concourse.bass as bass
import concourse.mybir as mybir
from concourse.bass_utils import run_bass_kernel_spmd

F32 = mybir.dt.float32
BF16 = mybir.dt.bfloat16
AF = mybir.ActivationFunctionType
ALU = mybir.AluOpType
AX = mybir.AxisListType
BF16NP = ml_dtypes.bfloat16

B, T, D = 2, 4096, 2048
NH, NKV, HD = 32, 8, 64
HID = 8192
NCORES = 8
NKV_COLS = 6144
NTILES = 48
NG = 12
QGROUPS = (7, 11)
S0_SLOTS, S1_SLOTS = 32, 16
S0_PAIRS, S1_PAIRS = 16, 8
NPAIRS = S0_PAIRS + S1_PAIRS
DC = 16
HC = 64
ROPE_BASE = 10000.0
EPS = 1e-5
FPH = 8
NPH = HC // FPH
KILLV = -30000.0


def head_perm():
    perm = []
    for gp in range(4):
        for i in range(4):
            for h in (8 * gp + i, 8 * gp + 4 + i):
                perm.extend(range(h * 64, (h + 1) * 64))
    return np.array(perm)

HEAD_PERM = head_perm()

def chunk_of_head(h):
    return (h // 8) * 4 + (h % 4)

def base_of_head(h):
    return 64 * ((h // 4) % 2)


class Sem:
    def __init__(self, handle):
        self.h = handle
        self.v = 0


class Prog:
    ENGINES = ("sync", "scalar", "vector", "tensor", "gpsimd")

    def __init__(self, nc, st):
        self.nc = nc
        self.ops = {e: [] for e in self.ENGINES}
        self.st = st
        self.sems = []

    def sem(self, name):
        s = Sem(self.st.enter_context(self.nc.semaphore(name)))
        self.sems.append(s)
        return s

    def W(self, eng, sem, val=None):
        v = sem.v if val is None else val
        if v > 0:
            self.ops[eng].append(lambda e, s=sem, vv=v: e.wait_ge(s.h, vv))

    def emit(self, eng, meth, *args, inc=None, **kw):
        if inc is not None:
            sem, n = inc
            sem.v += n
            def fn(e, m=meth, a=args, k=kw, s=sem, nn=n):
                getattr(e, m)(*a, **k).then_inc(s.h, nn)
            self.ops[eng].append(fn)
            return sem.v
        else:
            def fn(e, m=meth, a=args, k=kw):
                getattr(e, m)(*a, **k)
        self.ops[eng].append(fn)
        return None

    def dma(self, eng, dst, src, sem):
        sem.v += 16
        self.ops[eng].append(lambda e, d=dst, s=src, q=sem: e.dma_start(d, s).then_inc(q.h, 16))
        return sem.v

    def barrier(self):
        for eng in self.ENGINES:
            for s in self.sems:
                self.W(eng, s)


def hx(ap, c):
    return ap.rearrange("p (g c) -> p g c", c=c)


def build_nc(dump=None):
    nc = bass.Bass("TRN2", target_bir_lowering=False)
    st = ExitStack()
    p = Prog(nc, st)

    xt = nc.dram_tensor("xt", [NTILES, 128, 2048], F32, kind="ExternalInput").ap()
    xq = nc.dram_tensor("xq", [DC, 128, 1024], F32, kind="ExternalInput").ap()
    cosk = nc.dram_tensor("cosk", [NTILES, 128, 512], F32, kind="ExternalInput").ap()
    wq = nc.dram_tensor("wq", [4, DC, 128, 512], F32, kind="ExternalInput").ap()
    wk = nc.dram_tensor("wk", [DC, 128, 512], F32, kind="ExternalInput").ap()
    wv = nc.dram_tensor("wv", [DC, 128, 512], F32, kind="ExternalInput").ap()
    wo = nc.dram_tensor("wo", [DC, 128, 2048], F32, kind="ExternalInput").ap()
    wg = nc.dram_tensor("wg", [HC, 128, 2048], F32, kind="ExternalInput").ap()
    wu = nc.dram_tensor("wu", [HC, 128, 2048], F32, kind="ExternalInput").ap()
    wd = nc.dram_tensor("wd", [HC, 128, 2048], F32, kind="ExternalInput").ap()
    identf_in = nc.dram_tensor("identfin", [128, 128], F32, kind="ExternalInput").ap()
    masks_in = nc.dram_tensor("masksin", [4, 128, 512], BF16, kind="ExternalInput").ap()
    kill_in = nc.dram_tensor("killin", [1, NPAIRS], F32, kind="ExternalInput").ap()
    out_t = nc.dram_tensor("out", [DC, 128, 1024], F32, kind="ExternalOutput").ap()
    x2sp = nc.dram_tensor("x2spill", [DC, 128, 1024], F32).ap()
    dumps = {}
    if dump == "qkv":
        dumps["d_q"] = nc.dram_tensor("d_q", [128, 16 * 1024], BF16, kind="ExternalOutput").ap()
        dumps["d_k"] = nc.dram_tensor("d_k", [128, 4 * NKV_COLS], BF16, kind="ExternalOutput").ap()
        dumps["d_v"] = nc.dram_tensor("d_v", [128, 48 * 520], BF16, kind="ExternalOutput").ap()
        dumps["d_bias"] = nc.dram_tensor("d_bias", [128, 32 * NPAIRS], F32, kind="ExternalOutput").ap()
    if dump == "attn":
        dumps["d_ctx"] = nc.dram_tensor("d_ctx", [128, 16 * 1024], BF16, kind="ExternalOutput").ap()
    if dump == "x2":
        dumps["d_x2"] = nc.dram_tensor("d_x2", [128, 16 * 1024], BF16, kind="ExternalOutput").ap()

    qS, qA, qG = p.sem("qS"), p.sem("qA"), p.sem("qG")
    sPE, sDVE, sACT, sPOOL = p.sem("sPE"), p.sem("sDVE"), p.sem("sACT"), p.sem("sPOOL")

    identf = st.enter_context(nc.sbuf_tensor("identft", [128, 128], F32))
    ones1 = st.enter_context(nc.sbuf_tensor("ones1", [1, 128], F32))
    onesh = st.enter_context(nc.sbuf_tensor("onesh", [128, 2], BF16))
    ones128 = st.enter_context(nc.sbuf_tensor("ones128", [128, 1], BF16))
    epsrow = st.enter_context(nc.sbuf_tensor("epsrow", [1, 1], F32))
    rows = st.enter_context(nc.sbuf_tensor("rows", [1, 1152], F32))
    R_KILL, R_QQ, R_KK, R_KKT, R_MN, R_BR, R_SC = 0, 32, 96, 112, 128, 256, 1024
    R_M, R_R, R_B, R_T = 0, 512, 1024, 1536

    p.dma("sync", identf[:], identf_in, qS)
    p.dma("sync", rows[0:1, 0:NPAIRS], kill_in, qS)
    p.emit("gpsimd", "memset", ones1[:], 1.0)
    p.emit("gpsimd", "memset", onesh[:], 0.0)
    p.emit("gpsimd", "memset", onesh[0:64, 0:1], 1.0)
    p.emit("gpsimd", "memset", onesh[64:128, 1:2], 1.0)
    p.emit("gpsimd", "memset", ones128[:], 1.0)
    p.emit("gpsimd", "memset", epsrow[:], EPS, inc=(sPOOL, 1))

    PS01 = st.enter_context(nc.psum_tensor("PS01", [128, 1024], F32))
    PS23 = st.enter_context(nc.psum_tensor("PS23", [128, 1024], F32))
    P4 = st.enter_context(nc.psum_tensor("P4", [128, 512], F32))
    P5 = st.enter_context(nc.psum_tensor("P5", [128, 512], F32))
    P6 = st.enter_context(nc.psum_tensor("P6", [128, 512], F32))
    P7 = st.enter_context(nc.psum_tensor("P7", [128, 512], F32))
    CTXB = [P4, P5, P6, P7]

    scopeA = ExitStack()
    biasT = scopeA.enter_context(nc.sbuf_tensor("biasT", [128, 32 * NPAIRS], BF16))
    k_fm = scopeA.enter_context(nc.sbuf_tensor("k_fm", [128, 4 * NKV_COLS], BF16))
    v_aug = scopeA.enter_context(nc.sbuf_tensor("v_aug", [128, 48 * 520], BF16))
    q_fm = scopeA.enter_context(nc.sbuf_tensor("q_fm", [128, 16 * 1024], BF16))
    KFM = hx(k_fm[:], NKV_COLS)

    scopeQ = ExitStack()
    scopeQR = ExitStack()
    xtstg = scopeQ.enter_context(nc.sbuf_tensor("xtstg", [128, 2 * 1024], F32))
    x_bf = scopeQ.enter_context(nc.sbuf_tensor("x_bf", [128, 4 * 2048], BF16))
    wkv16 = scopeQR.enter_context(nc.sbuf_tensor("wkv16", [128, 16 * 1024], BF16, side="right"))
    wq16 = scopeQ.enter_context(nc.sbuf_tensor("wq16", [128, 2 * 512], BF16))
    wqstg = scopeQ.enter_context(nc.sbuf_tensor("wqstg", [128, 2 * 512], F32))
    cosb = scopeQ.enter_context(nc.sbuf_tensor("cosb", [128, 2 * 512], F32))
    ktm = scopeQ.enter_context(nc.sbuf_tensor("ktm", [128, 1280], F32))
    sqs = scopeQ.enter_context(nc.sbuf_tensor("sqs", [128, 512], BF16))

    p.emit("gpsimd", "memset", v_aug[:], 1.0, inc=(sPOOL, 1))
    p.barrier()

    BANK = [P4[:], P5[:], P6[:], P7[:], PS01[:, 0:512], PS01[:, 512:1024],
            PS23[:, 0:512], PS23[:, 512:1024]]
    bank_free = [(sPE, 0)] * 8

    def bank_wait(b):
        sem, val = bank_free[b]
        p.W("tensor", sem, val)

    g_xtstg = [0, 0]
    for dc in range(DC):
        sl = dc % 2
        p.W("sync", sACT, g_xtstg[sl])
        d0 = p.dma("sync", xtstg[:, sl * 1024:sl * 1024 + 512], wk[dc], qS)
        p.W("gpsimd", sACT, g_xtstg[sl])
        d1 = p.dma("gpsimd", xtstg[:, sl * 1024 + 512:(sl + 1) * 1024], wv[dc], qG)
        p.W("scalar", qS, d0)
        p.W("scalar", qG, d1)
        g_xtstg[sl] = p.emit("scalar", "activation", wkv16[:, dc * 1024:(dc + 1) * 1024],
                             xtstg[:, sl * 1024:(sl + 1) * 1024], AF.Copy, inc=(sACT, 1))

    def rope_emit(psrc, fdst, cos_off, scr):
        c_ap = hx(cosb[:, cos_off:cos_off + 256], 32)
        s_ap = hx(cosb[:, cos_off + 256:cos_off + 512], 32)
        x1 = hx(psrc, 64)[:, :, 0:32]
        x2 = hx(psrc, 64)[:, :, 32:64]
        o1 = hx(fdst, 64)[:, :, 0:32]
        o2 = hx(fdst, 64)[:, :, 32:64]
        t2 = hx(scr, 32)
        p.emit("vector", "tensor_tensor", t2, x2, s_ap, op=ALU.mult)
        p.emit("vector", "tensor_tensor", o1, x1, c_ap, op=ALU.mult)
        p.emit("vector", "tensor_tensor", o1, o1, t2, op=ALU.subtract)
        p.emit("vector", "tensor_tensor", t2, x1, s_ap, op=ALU.mult)
        p.emit("vector", "tensor_tensor", o2, x2, c_ap, op=ALU.mult)
        return p.emit("vector", "tensor_tensor", o2, o2, t2, op=ALU.add, inc=(sDVE, 1))

    x_used = [0, 0, 0, 0]
    x_cast = [0] * NTILES
    g_cosb = [0, 0]
    g_ktm = [0, 0]
    g_wq16 = [0, 0]
    g_wqstg = [0, 0]
    rope_val = [0] * NTILES
    kdone_val = [0] * NTILES

    def load_cast_tile(t):
        r = t % 4
        p.W("sync", sACT, g_xtstg[0])
        d0 = p.dma("sync", xtstg[:, 0:1024], xt[t][:, 0:1024], qS)
        p.W("gpsimd", sACT, g_xtstg[1])
        d1 = p.dma("gpsimd", xtstg[:, 1024:2048], xt[t][:, 1024:2048], qG)
        p.W("scalar", qS, d0)
        p.W("scalar", sPE, x_used[r])
        g_xtstg[0] = p.emit("scalar", "activation", x_bf[:, r * 2048:r * 2048 + 1024],
                            xtstg[:, 0:1024], AF.Copy, inc=(sACT, 1))
        p.W("scalar", qG, d1)
        g_xtstg[1] = p.emit("scalar", "activation", x_bf[:, r * 2048 + 1024:(r + 1) * 2048],
                            xtstg[:, 1024:2048], AF.Copy, inc=(sACT, 1))
        x_cast[t] = g_xtstg[1]

    def kv_tile(t):
        r = t % 4
        kb = t % 2
        vb = 4 + t % 2
        p.W("tensor", sACT, x_cast[t])
        bank_wait(kb)
        for dc in range(DC):
            kw = {"inc": (sPE, 1)} if dc == DC - 1 else {}
            kd = p.emit("tensor", "matmul", BANK[kb],
                        x_bf[:, r * 2048 + dc * 128: r * 2048 + (dc + 1) * 128],
                        wkv16[:, dc * 1024:dc * 1024 + 512],
                        start=(dc == 0), stop=(dc == DC - 1), **kw)
        kdone_val[t] = kd
        bank_wait(vb)
        for dc in range(DC):
            kw = {"inc": (sPE, 1)} if dc == DC - 1 else {}
            vd = p.emit("tensor", "matmul", BANK[vb],
                        x_bf[:, r * 2048 + dc * 128: r * 2048 + (dc + 1) * 128],
                        wkv16[:, dc * 1024 + 512:(dc + 1) * 1024],
                        start=(dc == 0), stop=(dc == DC - 1), **kw)
        x_used[r] = vd
        cs = t % 2
        p.W("scalar", sDVE, g_cosb[cs])
        cosd = p.dma("scalar", cosb[:, cs * 512:(cs + 1) * 512], cosk[t], qA)
        p.W("vector", sPE, kdone_val[t])
        p.W("vector", qA, cosd)
        p.W("vector", sPE, g_ktm[cs])
        rv = rope_emit(BANK[kb], ktm[:, cs * 512:(cs + 1) * 512], cs * 512,
                       ktm[:, 1024:1280])
        rope_val[t] = rv
        g_cosb[cs] = rv
        bank_free[kb] = (sDVE, rv)
        p.W("scalar", sPE, x_used[r])
        vdst = hx(v_aug[:, t * 520: t * 520 + 520], 65)[:, :, 0:64]
        ve = p.emit("scalar", "activation", vdst, hx(BANK[vb], 64), AF.Copy, inc=(sACT, 1))
        bank_free[vb] = (sACT, ve)

    def ktranspose_tile(t):
        cs = t % 2
        tb = 6 + t % 2
        p.W("tensor", sDVE, rope_val[t])
        bank_wait(tb)
        for c4 in range(4):
            kw = {"inc": (sPE, 1)} if c4 == 3 else {}
            td = p.emit("tensor", "transpose", BANK[tb][:, c4 * 128:(c4 + 1) * 128],
                        ktm[:, cs * 512 + c4 * 128: cs * 512 + (c4 + 1) * 128], identf[:],
                        **kw)
        g_ktm[cs] = td
        p.W("vector", sPE, td)
        kf = p.emit("vector", "tensor_copy",
                    KFM[:, :, t * 128:(t + 1) * 128], hx(BANK[tb], 128), inc=(sDVE, 1))
        bank_free[tb] = (sDVE, kf)

    def q_section(grp):
        s = 0 if grp == QGROUPS[0] else 1
        t0 = grp * 4
        QB = [2, 3, 4, 5]
        for qc in range(4):
            for dc in range(DC):
                sl = dc % 2
                p.W("sync", sDVE, g_wqstg[1])
                wd = p.dma("sync", wqstg[:, sl * 512:(sl + 1) * 512], wq[qc][dc], qS)
                p.W("vector", qS, wd)
                p.W("vector", sPE, g_wq16[1])
                cw = p.emit("vector", "tensor_copy", wq16[:, sl * 512:(sl + 1) * 512],
                            wqstg[:, sl * 512:(sl + 1) * 512], inc=(sDVE, 1))
                g_wqstg[1] = cw
                p.W("tensor", sDVE, cw)
                if dc == 0:
                    for tt in range(4):
                        bank_wait(QB[tt])
                for tt in range(4):
                    kw = {"inc": (sPE, 1)} if tt == 3 else {}
                    qm = p.emit("tensor", "matmul", BANK[QB[tt]],
                                x_bf[:, ((t0 + tt) % 4) * 2048 + dc * 128:
                                     ((t0 + tt) % 4) * 2048 + (dc + 1) * 128],
                                wq16[:, sl * 512:(sl + 1) * 512],
                                start=(dc == 0), stop=(dc == DC - 1), **kw)
                g_wq16[1] = qm
            for tt in range(4):
                t = t0 + tt
                cs = tt % 2
                tb = 6 + tt % 2
                p.W("scalar", sDVE, g_cosb[cs])
                cosd = p.dma("scalar", cosb[:, cs * 512:(cs + 1) * 512], cosk[t], qA)
                p.W("vector", sPE, qm)
                p.W("vector", qA, cosd)
                p.W("vector", sPE, max(g_ktm[0], g_ktm[1]))
                rv = rope_emit(BANK[QB[tt]], ktm[:, cs * 512:(cs + 1) * 512], cs * 512,
                               ktm[:, 1024:1280])
                g_cosb[cs] = rv
                bank_free[QB[tt]] = (sDVE, rv)
                p.W("tensor", sDVE, rv)
                bank_wait(tb)
                for c4 in range(4):
                    kw = {"inc": (sPE, 1)} if c4 == 3 else {}
                    td = p.emit("tensor", "transpose", BANK[tb][:, c4 * 128:(c4 + 1) * 128],
                                ktm[:, cs * 512 + c4 * 128: cs * 512 + (c4 + 1) * 128],
                                identf[:], **kw)
                g_ktm[cs] = td
                p.W("vector", sPE, td)
                qdst = hx(q_fm[:], 1024)[:, 4 * qc:4 * qc + 4,
                                         s * 512 + tt * 128: s * 512 + (tt + 1) * 128]
                qf = p.emit("vector", "tensor_copy", qdst, hx(BANK[tb], 128), inc=(sDVE, 1))
                bank_free[tb] = (sDVE, qf)
        qlast = sPE.v
        for tt in range(4):
            x_used[(t0 + tt) % 4] = qlast

    tr_done = 0
    for t in range(NTILES):
        load_cast_tile(t)
        kv_tile(t)
        while tr_done < t:
            ktranspose_tile(tr_done)
            tr_done += 1
        if t % 4 == 3 and (t // 4) in QGROUPS:
            while tr_done <= t:
                ktranspose_tile(tr_done)
                tr_done += 1
            q_section(t // 4)
    while tr_done < NTILES:
        ktranspose_tile(tr_done)
        tr_done += 1

    p.barrier()
    p.emit("vector", "memset", rows[0:1, R_KK:R_KK + 32], 0.0, inc=(sDVE, 1))

    g_sqs = [0, 0]
    g_nbk = [0, 0]
    nblk = [0]

    def norm_block(src, cells):
        sl = nblk[0] % 2
        nblk[0] += 1
        bp = (P6[0:1, :], P7[0:1, :]) if sl == 0 else (P4[0:1, :], P5[0:1, :])
        p.W("scalar", sPE, g_sqs[0])
        sqv = p.emit("scalar", "activation", sqs[:], src,
                     AF.Square, inc=(sACT, 1))
        p.W("tensor", sACT, sqv)
        p.W("tensor", sDVE, g_nbk[sl])
        for half in range(2):
            kw = {"inc": (sPE, 1)} if half == 1 else {}
            mmv = p.emit("tensor", "matmul", bp[half],
                         onesh[:, half:half + 1], sqs[:],
                         start=True, stop=True, **kw)
        g_sqs[0] = mmv
        p.W("vector", sPE, mmv)
        for half in range(2):
            mode, col = cells[half]
            if mode == "direct":
                rdv = p.emit("vector", "tensor_reduce", rows[0:1, col:col + 1],
                             bp[half], axis=AX.X, op=ALU.max,
                             **({"inc": (sDVE, 1)} if half == 1 else {}))
            else:
                rdv = p.emit("vector", "tensor_reduce",
                             rows[0:1, R_KKT + half:R_KKT + half + 1],
                             bp[half], axis=AX.X, op=ALU.max,
                             **({"inc": (sDVE, 1)} if half == 1 else {}))
        g_nbk[sl] = rdv
        if cells[0][0] == "runmax":
            for half in range(2):
                col = cells[half][1]
                p.emit("vector", "tensor_tensor", rows[0:1, col:col + 1],
                       rows[0:1, col:col + 1], rows[0:1, R_KKT + half:R_KKT + half + 1],
                       op=ALU.max, **({"inc": (sDVE, 1)} if half == 1 else {}))

    for kc in range(4):
        for cg in range(NG):
            s = 0 if cg < 8 else 1
            src = k_fm[:, kc * NKV_COLS + cg * 512: kc * NKV_COLS + (cg + 1) * 512]
            cells = [("runmax", R_KK + s * 8 + 2 * kc + half) for half in range(2)]
            norm_block(src, cells)
    for qc in range(DC):
        for s in range(2):
            src = q_fm[:, qc * 1024 + s * 512: qc * 1024 + (s + 1) * 512]
            gp, i = qc // 4, qc % 4
            cells = [("direct", R_QQ + s * 32 + 8 * gp + 4 * half + i) for half in range(2)]
            norm_block(src, cells)

    for s in range(2):
        for g in range(8):
            p.emit("vector", "tensor_scalar",
                   rows[0:1, R_SC + s * 32 + 4 * g: R_SC + s * 32 + 4 * g + 4],
                   rows[0:1, R_QQ + s * 32 + 4 * g: R_QQ + s * 32 + 4 * g + 4],
                   rows[0:1, R_KK + s * 8 + g: R_KK + s * 8 + g + 1], None, op0=ALU.mult,
                   **({"inc": (sDVE, 1)} if (s == 1 and g == 7) else {}))
    p.W("scalar", sDVE)
    p.emit("scalar", "activation", rows[0:1, R_SC + 64:R_SC + 128], rows[0:1, R_SC:R_SC + 64],
           AF.Sqrt, inc=(sACT, 1))
    p.W("vector", sACT)
    p.emit("vector", "tensor_scalar", rows[0:1, R_MN:R_MN + 64],
           rows[0:1, R_SC + 64:R_SC + 128], -0.125, None, op0=ALU.mult, inc=(sDVE, 1))
    for h in range(NH):
        for s in range(2):
            off = 0 if s == 0 else S0_PAIRS
            n = S0_PAIRS if s == 0 else S1_PAIRS
            p.emit("vector", "tensor_scalar",
                   rows[0:1, R_BR + h * NPAIRS + off: R_BR + h * NPAIRS + off + n],
                   rows[0:1, R_KILL + off: R_KILL + off + n],
                   rows[0:1, R_MN + s * 32 + h: R_MN + s * 32 + h + 1], None, op0=ALU.add,
                   **({"inc": (sDVE, 1)} if (h == NH - 1 and s == 1) else {}))
    p.W("tensor", sDVE)
    p.emit("tensor", "matmul", PS01[:, 0:512], ones1[:], rows[0:1, R_BR:R_BR + 512],
           start=True, stop=True)
    p.emit("tensor", "matmul", PS01[:, 512:768], ones1[:], rows[0:1, R_BR + 512:R_BR + 768],
           start=True, stop=True, inc=(sPE, 1))
    p.W("vector", sPE)
    p.emit("vector", "tensor_copy", biasT[:, 0:512], PS01[:, 0:512])
    p.emit("vector", "tensor_copy", biasT[:, 512:768], PS01[:, 512:768], inc=(sDVE, 1))
    p.barrier()

    def finish_and_replay(close_scopes=()):
        for q in (qS, qA, qG):
            p.W("gpsimd", q)
        bst = ExitStack()
        blk = bst.enter_context(nc.Block())
        for eng in Prog.ENGINES:
            ops = p.ops[eng]
            getattr(blk, eng)(lambda e, ops=ops: [fn(e) for fn in ops])
        bst.close()
        for sc in close_scopes:
            sc.close()
        st.close()

    if dump == "qkv":
        p.dma("gpsimd", dumps["d_q"], q_fm[:], qG)
        p.dma("gpsimd", dumps["d_k"], k_fm[:], qG)
        p.dma("gpsimd", dumps["d_v"], v_aug[:], qG)
        p.dma("gpsimd", dumps["d_bias"], biasT[:], qG)
        finish_and_replay([scopeQ, scopeQR, scopeA])
        return nc, dumps

    scopeQ.close()
    scopeQR.close()
    scopeR = ExitStack()
    ctx = scopeR.enter_context(nc.sbuf_tensor("ctxt", [128, 16 * 1024], BF16, side="right"))
    scopeT = ExitStack()
    masks = scopeT.enter_context(nc.sbuf_tensor("maskst", [128, 4 * 512], BF16))
    attn_t = scopeT.enter_context(nc.sbuf_tensor("attn_t", [128, 3 * 1024], BF16))
    brow = scopeT.enter_context(nc.sbuf_tensor("brow", [128, 2 * 512], F32))
    bcsb = scopeT.enter_context(nc.sbuf_tensor("bcsb", [128, 2 * 512], F32))

    for i in range(4):
        p.dma("sync", masks[:, i * 512:(i + 1) * 512], masks_in[i], qS)
    mask_ready = qS.v
    p.W("vector", qS, mask_ready)

    plist = []
    for s in range(2):
        npr = S0_PAIRS if s == 0 else S1_PAIRS
        for g in range(8):
            for i in range(4):
                for j2 in range(npr):
                    plist.append((s, g, i, j2))
    NP = len(plist)
    SC2 = [PS01, PS23]
    scb_guard = [0, 0]
    ab_guard = [0, 0, 0]
    ctxb_guard = [0, 0, 0, 0]
    g_brow = [0, 0]
    g_bcsb = [0, 0]
    exp_sem = [None] * NP
    exp_val = [0] * NP
    rot = [0]
    nbq = [0]
    normq = []

    def emit_norm(ent):
        s_, g_, i_, _due, bslot, recv = ent
        h = 4 * g_ + i_
        qcc, hb2 = chunk_of_head(h), base_of_head(h)
        m = rot[0] % 2
        rot[0] += 1
        p.W("tensor", sACT, scb_guard[m])
        p.W("tensor", sDVE, recv)
        bcv = p.emit("tensor", "matmul", SC2[m][:, 0:512], ones1[:],
                     brow[0:1, bslot * 512:bslot * 512 + 512],
                     start=True, stop=True, inc=(sPE, 1))
        g_brow[bslot] = bcv
        p.W("scalar", sPE, bcv)
        p.W("scalar", sDVE, g_bcsb[bslot])
        bcsbv = p.emit("scalar", "activation", bcsb[:, bslot * 512:(bslot + 1) * 512],
                       SC2[m][:, 0:512], AF.Copy, inc=(sACT, 1))
        scb_guard[m] = bcsbv
        p.W("vector", sACT, bcsbv)
        cdst = ctx[hb2:hb2 + 64, qcc * 1024 + s_ * 512: qcc * 1024 + (s_ + 1) * 512]
        cv = p.emit("vector", "tensor_tensor", cdst, CTXB[i_][0:64, :],
                    bcsb[0:64, bslot * 512:(bslot + 1) * 512], op=ALU.mult, inc=(sDVE, 1))
        ctxb_guard[i_] = cv
        g_bcsb[bslot] = cv

    cv_last = [0]
    for step in range(NP + 2):
        while normq and normq[0][3] <= step:
            emit_norm(normq.pop(0))
        cidx = step - 2
        if 0 <= cidx < NP:
            s_, g_, i_, j2 = plist[cidx]
            npr = S0_PAIRS if s_ == 0 else S1_PAIRS
            kslot0 = 0 if s_ == 0 else S0_SLOTS
            aslot = cidx % 3
            ab = attn_t[:, aslot * 1024:(aslot + 1) * 1024]
            p.W("tensor", exp_sem[cidx], exp_val[cidx])
            if j2 == 0:
                p.W("tensor", sDVE, ctxb_guard[i_])
            for dj in range(2):
                kslot = kslot0 + 2 * j2 + dj
                cv = p.emit("tensor", "matmul", CTXB[i_][0:65, :],
                            v_aug[:, kslot * 520 + g_ * 65: kslot * 520 + (g_ + 1) * 65],
                            ab[:, dj * 512:(dj + 1) * 512],
                            start=(j2 == 0 and dj == 0),
                            stop=(j2 == npr - 1 and dj == 1),
                            **({"inc": (sPE, 1)} if dj == 1 else {}))
            cv_last[0] = cv
            if j2 == npr - 1:
                bslot = nbq[0] % 2
                nbq[0] += 1
                p.W("vector", sPE, max(cv, g_brow[bslot]))
                p.emit("vector", "tensor_copy", brow[0:1, bslot * 512:bslot * 512 + 512],
                       CTXB[i_][64:65, :])
                recv = p.emit("vector", "reciprocal", brow[0:1, bslot * 512:bslot * 512 + 512],
                              brow[0:1, bslot * 512:bslot * 512 + 512], inc=(sDVE, 1))
                normq.append((s_, g_, i_, step + 5, bslot, recv))
        if step < NP:
            s_, g_, i_, j2 = plist[step]
            npr = S0_PAIRS if s_ == 0 else S1_PAIRS
            kslot0 = 0 if s_ == 0 else S0_SLOTS
            kc, half = g_ // 2, g_ % 2
            hb = 64 * half
            h = 4 * g_ + i_
            qcc = chunk_of_head(h)
            m = rot[0] % 2
            rot[0] += 1
            pb = SC2[m]
            aslot = step % 3
            ab = attn_t[:, aslot * 1024:(aslot + 1) * 1024]
            p.W("tensor", sACT, scb_guard[m])
            for dj in range(2):
                kslot = kslot0 + 2 * j2 + dj
                scv = p.emit("tensor", "matmul", pb[:, dj * 512:(dj + 1) * 512],
                             k_fm[hb:hb + 64,
                                  kc * NKV_COLS + kslot * 128: kc * NKV_COLS + (kslot + 1) * 128],
                             q_fm[hb:hb + 64,
                                  qcc * 1024 + s_ * 512: qcc * 1024 + (s_ + 1) * 512],
                             start=True, stop=True,
                             **({"inc": (sPE, 1)} if dj == 1 else {}))
            bcol = h * NPAIRS + (j2 if s_ == 0 else S0_PAIRS + j2)
            p.W("scalar", sPE, max(scv, cv_last[0]))
            ev = p.emit("scalar", "activation", ab, pb[:], AF.Exp,
                        bias=biasT[:, bcol:bcol + 1], scale=0.125, inc=(sACT, 1))
            scb_guard[m] = ev
            exp_sem[step], exp_val[step] = sACT, ev
            if j2 >= npr - 2:
                p.W("vector", sACT, ev)
                dj0 = 2 * (j2 - (npr - 2))
                p.emit("vector", "tensor_tensor", ab[:, 0:512], ab[:, 0:512],
                       masks[:, dj0 * 512:(dj0 + 1) * 512], op=ALU.mult)
                mv = p.emit("vector", "tensor_tensor", ab[:, 512:1024], ab[:, 512:1024],
                            masks[:, (dj0 + 1) * 512:(dj0 + 2) * 512], op=ALU.mult,
                            inc=(sDVE, 1))
                exp_sem[step], exp_val[step] = sDVE, mv
    while normq:
        emit_norm(normq.pop(0))

    scopeT.close()
    p.barrier()
    if dump == "attn":
        p.dma("gpsimd", dumps["d_ctx"], ctx[:], qG)
        finish_and_replay([scopeA, scopeR])
        return nc, dumps

    scopeA.close()
    x2b = st.enter_context(nc.sbuf_tensor("x2bt", [128, 16 * 1024], BF16))
    lnrows = st.enter_context(nc.sbuf_tensor("lnrows", [1, 2048], F32))
    scopeW = ExitStack()
    wostg = scopeW.enter_context(nc.sbuf_tensor("wostg", [128, 2 * 2048], F32))
    wo16 = scopeW.enter_context(nc.sbuf_tensor("wo16", [128, 2 * 2048], BF16))
    xrst = scopeW.enter_context(nc.sbuf_tensor("xrst", [128, 4 * 512], F32))
    s1f = scopeW.enter_context(nc.sbuf_tensor("s1f", [128, 32 * 512], F32))
    sc16 = scopeW.enter_context(nc.sbuf_tensor("sc16", [128, 4 * 512], BF16))
    sq16 = scopeW.enter_context(nc.sbuf_tensor("sq16", [128, 4 * 512], BF16))
    rbc = scopeW.enter_context(nc.sbuf_tensor("rbc", [128, 512], F32))
    bbc = scopeW.enter_context(nc.sbuf_tensor("bbc", [128, 512], F32))
    tsc = scopeW.enter_context(nc.sbuf_tensor("tsc", [128, 512], F32))
    x2f = scopeW.enter_context(nc.sbuf_tensor("x2f", [128, 2 * 512], F32))

    def ln_rows(sum_ps, sq_ps, r_dst, b_dst, ready):
        p.W("vector", sPE, ready)
        p.emit("vector", "tensor_scalar", lnrows[0:1, R_M:R_M + 512], sum_ps, 1.0 / D, None,
               op0=ALU.mult)
        p.emit("vector", "tensor_scalar", lnrows[0:1, R_T:R_T + 512], sq_ps, 1.0 / D, None,
               op0=ALU.mult)
        p.emit("vector", "tensor_tensor", lnrows[0:1, R_B:R_B + 512],
               lnrows[0:1, R_M:R_M + 512], lnrows[0:1, R_M:R_M + 512], op=ALU.mult)
        v1 = p.emit("vector", "tensor_tensor", lnrows[0:1, R_T:R_T + 512],
               lnrows[0:1, R_T:R_T + 512], lnrows[0:1, R_B:R_B + 512], op=ALU.subtract,
               inc=(sDVE, 1))
        p.W("scalar", sDVE, v1)
        a1 = p.emit("scalar", "activation", lnrows[0:1, R_T:R_T + 512], lnrows[0:1, R_T:R_T + 512],
               AF.Sqrt, bias=epsrow[0:1, 0:1], inc=(sACT, 1))
        p.W("vector", sACT, a1)
        p.emit("vector", "reciprocal", lnrows[0:1, R_R:R_R + 512], lnrows[0:1, R_T:R_T + 512])
        v2 = p.emit("vector", "tensor_tensor", lnrows[0:1, R_B:R_B + 512],
               lnrows[0:1, R_M:R_M + 512], lnrows[0:1, R_R:R_R + 512], op=ALU.mult,
               inc=(sDVE, 1))
        p.W("tensor", sDVE, v2)
        p.emit("tensor", "matmul", PS01[:, 0:512], ones1[:], lnrows[0:1, R_R:R_R + 512],
               start=True, stop=True)
        m1 = p.emit("tensor", "matmul", PS01[:, 512:1024], ones1[:], lnrows[0:1, R_B:R_B + 512],
               start=True, stop=True, inc=(sPE, 1))
        p.W("vector", sPE, m1)
        p.emit("vector", "tensor_copy", r_dst, PS01[:, 0:512])
        return p.emit("vector", "tensor_copy", b_dst, PS01[:, 512:1024], inc=(sDVE, 1))

    g_wostg = [0, 0]
    g_wo16 = [0, 0]
    g_psB = [0, 0, 0, 0]
    g_xr = [0, 0, 0, 0]
    g_sc = [0, 0, 0, 0]
    stat_ready = [0, 0]
    WOUT = [P4, P5, P6, P7]
    STAT = [(PS01[0:1, 0:512], PS01[0:1, 512:1024]),
            (PS23[0:1, 0:512], PS23[0:1, 512:1024])]
    for dc in range(DC):
        buf = dc % 2
        p.W("sync", sDVE, g_wostg[buf])
        wodma = p.dma("sync", wostg[:, buf * 2048:(buf + 1) * 2048], wo[dc], qS)
        p.W("vector", qS, wodma)
        p.W("vector", sPE, g_wo16[buf])
        g_wostg[buf] = p.emit("vector", "tensor_copy", wo16[:, buf * 2048:(buf + 1) * 2048],
                              wostg[:, buf * 2048:(buf + 1) * 2048], inc=(sDVE, 1))
        for s in range(2):
            slot = buf * 2 + s
            p.W("scalar", sDVE, g_xr[slot])
            xrdma = p.dma("scalar", xrst[:, slot * 512:(slot + 1) * 512],
                          xq[dc][:, s * 512:(s + 1) * 512], qA)
            ap_ = WOUT[slot]
            p.W("tensor", sDVE, max(g_wostg[buf], g_psB[slot]))
            for cc in range(DC):
                kw = {"inc": (sPE, 1)} if cc == DC - 1 else {}
                mmv = p.emit("tensor", "matmul", ap_[:],
                             wo16[:, buf * 2048 + cc * 128: buf * 2048 + (cc + 1) * 128],
                             ctx[:, cc * 1024 + s * 512: cc * 1024 + (s + 1) * 512],
                             start=(cc == 0), stop=(cc == DC - 1), **kw)
            g_wo16[buf] = mmv
            p.W("vector", sPE, max(mmv, g_sc[slot]))
            p.W("vector", qA, xrdma)
            p.emit("vector", "tensor_tensor", s1f[:, (dc * 2 + s) * 512:(dc * 2 + s + 1) * 512],
                   ap_[:], xrst[:, slot * 512:(slot + 1) * 512], op=ALU.add)
            g_psB[slot] = p.emit("vector", "tensor_copy", sc16[:, slot * 512:(slot + 1) * 512],
                                 s1f[:, (dc * 2 + s) * 512:(dc * 2 + s + 1) * 512],
                                 inc=(sDVE, 1))
            g_xr[slot] = g_psB[slot]
            p.W("scalar", sDVE, g_psB[slot])
            sqv = p.emit("scalar", "activation", sq16[:, slot * 512:(slot + 1) * 512],
                         s1f[:, (dc * 2 + s) * 512:(dc * 2 + s + 1) * 512],
                         AF.Square, inc=(sACT, 1))
            p.W("tensor", sACT, sqv)
            sps, sqs_ = STAT[s]
            p.emit("tensor", "matmul", sps, ones128[:], sc16[:, slot * 512:(slot + 1) * 512],
                   start=(dc == 0), stop=(dc == DC - 1))
            g_sc[slot] = p.emit("tensor", "matmul", sqs_, ones128[:],
                                sq16[:, slot * 512:(slot + 1) * 512],
                                start=(dc == 0), stop=(dc == DC - 1), inc=(sPE, 1))
            if dc == DC - 1:
                stat_ready[s] = g_sc[slot]
    g_x2f = [0, 0]
    x2b_ready = [0, 0]
    for s in range(2):
        lnv = ln_rows(STAT[s][0], STAT[s][1], rbc[:], bbc[:], stat_ready[s])
        for dc in range(DC):
            buf = dc % 2
            p.W("vector", qG, g_x2f[buf])
            p.emit("vector", "tensor_tensor", tsc[:],
                   s1f[:, (dc * 2 + s) * 512:(dc * 2 + s + 1) * 512], rbc[:], op=ALU.mult)
            p.emit("vector", "tensor_tensor", x2f[:, buf * 512:(buf + 1) * 512], tsc[:], bbc[:],
                   op=ALU.subtract)
            x2v = p.emit("vector", "tensor_copy",
                   x2b[:, dc * 1024 + s * 512: dc * 1024 + (s + 1) * 512],
                   x2f[:, buf * 512:(buf + 1) * 512], inc=(sDVE, 1))
            p.W("gpsimd", sDVE, x2v)
            g_x2f[buf] = p.dma("gpsimd", x2sp[dc][:, s * 512:(s + 1) * 512],
                               x2f[:, buf * 512:(buf + 1) * 512], qG)
        x2b_ready[s] = x2v

    scopeW.close()
    scopeR.close()
    p.barrier()
    if dump == "x2":
        p.dma("gpsimd", dumps["d_x2"], x2b[:], qG)
        finish_and_replay()
        return nc, dumps

    scopeF = ExitStack()
    yacc = scopeF.enter_context(nc.sbuf_tensor("yacc", [128, 16 * 1024], F32))
    scopeFI = ExitStack()
    wgstg = scopeFI.enter_context(nc.sbuf_tensor("wgstg", [128, 2 * 1024], F32))
    wustg = scopeFI.enter_context(nc.sbuf_tensor("wustg", [128, 2 * 1024], F32))
    wdstg = scopeFI.enter_context(nc.sbuf_tensor("wdstg", [128, 2 * 1024], F32))
    wg16 = scopeFI.enter_context(nc.sbuf_tensor("wg16", [128, 2 * 2048], BF16))
    wu16 = scopeFI.enter_context(nc.sbuf_tensor("wu16", [128, 2 * 2048], BF16))
    wd16 = scopeFI.enter_context(nc.sbuf_tensor("wd16", [128, FPH * 2048], BF16))
    gated = scopeFI.enter_context(nc.sbuf_tensor("gated", [128, FPH * 1024], BF16))
    silu = scopeFI.enter_context(nc.sbuf_tensor("silu", [128, 2 * 512], F32))
    x2rs = scopeFI.enter_context(nc.sbuf_tensor("x2rs", [128, 2 * 512], F32))
    YB = [PS01[:, 0:512], PS01[:, 512:1024], PS23[:, 0:512], PS23[:, 512:1024]]
    GB = [(P4, P5), (P6, P7)]

    g_stg = [[0, 0], [0, 0], [0, 0]]
    g_w16 = [0, 0]
    g_gb = [0, 0]
    g_ub = [0, 0]
    g_yb = [0, 0, 0, 0]
    g_wd_used = 0
    g_x2rs = [0, 0]
    for ph in range(NPH):
        cast_last = 0
        gated_ready = [0, 0]
        for hxi in range(FPH):
            hc = ph * FPH + hxi
            buf = hc % 2
            dmav = [[0, 0], [0, 0], [0, 0]]
            for half in range(2):
                sl = (hc * 2 + half) % 2
                p.W("sync", sDVE, g_stg[0][sl])
                dmav[0][half] = p.dma("sync", wgstg[:, sl * 1024:(sl + 1) * 1024],
                                      wg[hc][:, half * 1024:(half + 1) * 1024], qS)
                p.W("scalar", sDVE, g_stg[1][sl])
                dmav[1][half] = p.dma("scalar", wustg[:, sl * 1024:(sl + 1) * 1024],
                                      wu[hc][:, half * 1024:(half + 1) * 1024], qA)
                p.W("gpsimd", sDVE, g_stg[2][sl])
                dmav[2][half] = p.dma("gpsimd", wdstg[:, sl * 1024:(sl + 1) * 1024],
                                      wd[hc][:, half * 1024:(half + 1) * 1024], qG)
            for half in range(2):
                sl = (hc * 2 + half) % 2
                p.W("vector", qS, dmav[0][half])
                p.W("vector", sPE, g_w16[buf])
                g_stg[0][sl] = p.emit("vector", "tensor_copy",
                                      wg16[:, buf * 2048 + half * 1024: buf * 2048 + (half + 1) * 1024],
                                      wgstg[:, sl * 1024:(sl + 1) * 1024], inc=(sDVE, 1))
                p.W("vector", qA, dmav[1][half])
                g_stg[1][sl] = p.emit("vector", "tensor_copy",
                                      wu16[:, buf * 2048 + half * 1024: buf * 2048 + (half + 1) * 1024],
                                      wustg[:, sl * 1024:(sl + 1) * 1024], inc=(sDVE, 1))
                p.W("vector", qG, dmav[2][half])
                p.W("vector", sPE, g_wd_used)
                g_stg[2][sl] = p.emit("vector", "tensor_copy",
                                      wd16[:, hxi * 2048 + half * 1024: hxi * 2048 + (half + 1) * 1024],
                                      wdstg[:, sl * 1024:(sl + 1) * 1024], inc=(sDVE, 1))
            cast_gu = g_stg[1][(hc * 2 + 1) % 2]
            cast_last = g_stg[2][(hc * 2 + 1) % 2]
            for qs in range(2):
                mb = (hc * 2 + qs) % 2
                gb, ub = GB[mb]
                p.W("tensor", sDVE, max(cast_gu, x2b_ready[qs], g_ub[mb]))
                p.W("tensor", sACT, g_gb[mb])
                for dc in range(DC):
                    kw = {"inc": (sPE, 1)} if dc == DC - 1 else {}
                    gatev = p.emit("tensor", "matmul", gb[:],
                           wg16[:, buf * 2048 + dc * 128: buf * 2048 + (dc + 1) * 128],
                           x2b[:, dc * 1024 + qs * 512: dc * 1024 + (qs + 1) * 512],
                           start=(dc == 0), stop=(dc == DC - 1), **kw)
                for dc in range(DC):
                    kw = {"inc": (sPE, 1)} if dc == DC - 1 else {}
                    upv = p.emit("tensor", "matmul", ub[:],
                           wu16[:, buf * 2048 + dc * 128: buf * 2048 + (dc + 1) * 128],
                           x2b[:, dc * 1024 + qs * 512: dc * 1024 + (qs + 1) * 512],
                           start=(dc == 0), stop=(dc == DC - 1), **kw)
                g_w16[buf] = upv
                p.W("scalar", sPE, gatev)
                siluv = p.emit("scalar", "activation", silu[:, mb * 512:(mb + 1) * 512],
                               gb[:], AF.Silu, inc=(sACT, 1))
                g_gb[mb] = siluv
                p.W("vector", sACT, siluv)
                p.W("vector", sPE, max(upv, g_wd_used))
                multv = p.emit("vector", "tensor_tensor",
                       gated[:, hxi * 1024 + qs * 512: hxi * 1024 + (qs + 1) * 512],
                       silu[:, mb * 512:(mb + 1) * 512], ub[:], op=ALU.mult, inc=(sDVE, 1))
                g_ub[mb] = multv
                gated_ready[qs] = multv
        for dc in range(DC):
            for qs in range(2):
                ybi = (dc * 2 + qs) % 4
                yb = YB[ybi]
                p.W("tensor", sDVE, max(gated_ready[qs], cast_last, g_yb[ybi]))
                for hxi in range(FPH):
                    kw = {"inc": (sPE, 1)} if hxi == FPH - 1 else {}
                    ydone = p.emit("tensor", "matmul", yb,
                           wd16[:, hxi * 2048 + dc * 128: hxi * 2048 + (dc + 1) * 128],
                           gated[:, hxi * 1024 + qs * 512: hxi * 1024 + (qs + 1) * 512],
                           start=(hxi == 0), stop=(hxi == FPH - 1), **kw)
                ycol = dc * 1024 + qs * 512
                p.W("vector", sPE, ydone)
                if ph == 0:
                    g_yb[ybi] = p.emit("vector", "tensor_copy", yacc[:, ycol:ycol + 512], yb,
                           inc=(sDVE, 1))
                elif ph < NPH - 1:
                    g_yb[ybi] = p.emit("vector", "tensor_tensor", yacc[:, ycol:ycol + 512],
                           yacc[:, ycol:ycol + 512], yb, op=ALU.add, inc=(sDVE, 1))
                else:
                    sb_ = (dc * 2 + qs) % 2
                    p.W("scalar", sDVE, g_x2rs[sb_])
                    xrv = p.dma("scalar", x2rs[:, sb_ * 512:(sb_ + 1) * 512],
                          x2sp[dc][:, qs * 512:(qs + 1) * 512], qA)
                    p.emit("vector", "tensor_tensor", yacc[:, ycol:ycol + 512],
                           yacc[:, ycol:ycol + 512], yb, op=ALU.add)
                    p.W("vector", qA, xrv)
                    g_yb[ybi] = p.emit("vector", "tensor_tensor", yacc[:, ycol:ycol + 512],
                           yacc[:, ycol:ycol + 512], x2rs[:, sb_ * 512:(sb_ + 1) * 512],
                           op=ALU.add, inc=(sDVE, 1))
                    g_x2rs[sb_] = g_yb[ybi]
        g_wd_used = sPE.v

    p.barrier()
    scopeFI.close()
    scopeL = ExitStack()
    sc16b = scopeL.enter_context(nc.sbuf_tensor("sc16b", [128, 4 * 512], BF16))
    sq16b = scopeL.enter_context(nc.sbuf_tensor("sq16b", [128, 4 * 512], BF16))
    rbc2 = scopeL.enter_context(nc.sbuf_tensor("rbc2", [128, 512], F32))
    bbc2 = scopeL.enter_context(nc.sbuf_tensor("bbc2", [128, 512], F32))
    ost = scopeL.enter_context(nc.sbuf_tensor("ost", [128, 2 * 512], F32))

    g_sc6 = [0, 0, 0, 0]
    stat6_ready = [0, 0]
    for dc in range(DC):
        for qs in range(2):
            slot = (dc % 2) * 2 + qs
            ycol = dc * 1024 + qs * 512
            p.W("vector", sPE, g_sc6[slot])
            scv = p.emit("vector", "tensor_copy", sc16b[:, slot * 512:(slot + 1) * 512],
                   yacc[:, ycol:ycol + 512], inc=(sDVE, 1))
            p.W("scalar", sPE, g_sc6[slot])
            sqv = p.emit("scalar", "activation", sq16b[:, slot * 512:(slot + 1) * 512],
                   yacc[:, ycol:ycol + 512], AF.Square, inc=(sACT, 1))
            p.W("tensor", sDVE, scv)
            p.W("tensor", sACT, sqv)
            p.emit("tensor", "matmul", (P6 if qs == 0 else P7)[0:1, :], ones128[:],
                   sc16b[:, slot * 512:(slot + 1) * 512], start=(dc == 0), stop=(dc == DC - 1))
            g_sc6[slot] = p.emit("tensor", "matmul", (P4 if qs == 0 else P5)[0:1, :], ones128[:],
                   sq16b[:, slot * 512:(slot + 1) * 512], start=(dc == 0), stop=(dc == DC - 1),
                   inc=(sPE, 1))
            if dc == DC - 1:
                stat6_ready[qs] = g_sc6[slot]
    g_ost = [0, 0]
    for qs in range(2):
        ln_rows((P6 if qs == 0 else P7)[0:1, :], (P4 if qs == 0 else P5)[0:1, :],
                rbc2[:], bbc2[:], stat6_ready[qs])
        for dc in range(DC):
            buf = dc % 2
            ycol = dc * 1024 + qs * 512
            p.W("vector", qG, g_ost[buf])
            p.emit("vector", "tensor_tensor", ost[:, buf * 512:(buf + 1) * 512],
                   yacc[:, ycol:ycol + 512], rbc2[:], op=ALU.mult)
            ov = p.emit("vector", "tensor_tensor", ost[:, buf * 512:(buf + 1) * 512],
                   ost[:, buf * 512:(buf + 1) * 512], bbc2[:], op=ALU.subtract,
                   inc=(sDVE, 1))
            p.W("gpsimd", sDVE, ov)
            g_ost[buf] = p.dma("gpsimd", out_t[dc][:, qs * 512:(qs + 1) * 512],
                  ost[:, buf * 512:(buf + 1) * 512], qG)

    scopeL.close()
    scopeF.close()
    finish_and_replay()
    return nc, dumps



_CACHE = {}

def _core_meta(k):
    kk = k % 4
    c0, c1 = 4 + kk, 3 - kk
    bA = 0 if k < 4 else 1
    return c0, c1, bA, 1 - bA


def _rope_tables():
    theta = 1.0 / ROPE_BASE ** (np.arange(0, HD, 2, dtype=np.float64) / HD)
    pos = np.arange(T, dtype=np.float64)
    fr = np.outer(pos, theta)
    return np.cos(fr).astype(np.float32), np.sin(fr).astype(np.float32)


def _prep_shared(Wq, Wk, Wv, Wo, Wg, Wu, Wd):
    wq_p = np.ascontiguousarray(Wq[:, HEAD_PERM])
    wq_t = np.ascontiguousarray(wq_p.reshape(DC, 128, 4, 512).transpose(2, 0, 1, 3))
    wk_t = np.ascontiguousarray(Wk.reshape(DC, 128, 512))
    wv_t = np.ascontiguousarray(Wv.reshape(DC, 128, 512))
    wo_p = Wo[HEAD_PERM, :]
    wo_t = np.ascontiguousarray(
        wo_p.reshape(DC, 128, DC, 128).transpose(2, 1, 0, 3).reshape(DC, 128, 2048))
    wgT = np.ascontiguousarray(Wg.T)
    wg_t = np.ascontiguousarray(
        wgT.reshape(DC, 128, HC, 128).transpose(2, 1, 0, 3).reshape(HC, 128, 2048))
    wuT = np.ascontiguousarray(Wu.T)
    wu_t = np.ascontiguousarray(
        wuT.reshape(DC, 128, HC, 128).transpose(2, 1, 0, 3).reshape(HC, 128, 2048))
    wdT = np.ascontiguousarray(Wd.T)
    wd_t = np.ascontiguousarray(wdT.reshape(HC, 128, D))
    ident = np.eye(128, dtype=BF16NP)
    m = np.zeros((4, 128, 512), dtype=BF16NP)
    ki = np.arange(128)[:, None]
    qi = np.arange(512)[None, :]
    for j in range(4):
        m[j] = (128 * j + ki <= qi).astype(BF16NP)
    return dict(wq=wq_t, wk=wk_t, wv=wv_t, wo=wo_t, wg=wg_t, wu=wu_t, wd=wd_t,
                identfin=np.eye(128, dtype=np.float32), masksin=m)


def _prep_core(k, x, cosT, sinT):
    c0, c1, bA, bB = _core_meta(k)
    n0, n1 = 4 * (c0 + 1), 4 * (c1 + 1)
    xf = np.zeros((D, NKV_COLS), dtype=np.float32)
    posv = np.zeros(NKV_COLS, dtype=np.int64)
    xA, xB = x[bA].T, x[bB].T
    ni = 512 * c0
    xf[:, 0:ni] = xA[:, 0:ni]
    posv[0:ni] = np.arange(ni)
    xf[:, 3584:4096] = xA[:, ni:ni + 512]
    posv[3584:4096] = np.arange(ni, ni + 512)
    ni1 = 512 * c1
    xf[:, 4096:4096 + ni1] = xB[:, 0:ni1]
    posv[4096:4096 + ni1] = np.arange(ni1)
    xf[:, 5632:6144] = xB[:, ni1:ni1 + 512]
    posv[5632:6144] = np.arange(ni1, ni1 + 512)
    xt_a = np.ascontiguousarray(
        xf.reshape(DC, 128, NTILES, 128).transpose(2, 1, 0, 3).reshape(NTILES, 128, 2048))
    xq_a = np.ascontiguousarray(
        np.concatenate([xf[:, 3584:4096], xf[:, 5632:6144]], axis=1).reshape(DC, 128, 1024))
    ck = np.zeros((NTILES, 128, 512), dtype=np.float32)
    ct = cosT[posv].reshape(NTILES, 128, 32)
    stl = sinT[posv].reshape(NTILES, 128, 32)
    ck[:, :, 0:256] = np.tile(ct, (1, 1, 8))
    ck[:, :, 256:512] = np.tile(stl, (1, 1, 8))
    kill = np.zeros((1, NPAIRS), dtype=np.float32)
    for pr in range(S0_PAIRS):
        if n0 - 4 <= 2 * pr < S0_SLOTS - 4:
            kill[0, pr] = KILLV
    for pr in range(S1_PAIRS):
        if n1 - 4 <= 2 * pr < S1_SLOTS - 4:
            kill[0, S0_PAIRS + pr] = KILLV
    return dict(xt=xt_a, xq=xq_a, cosk=ck, killin=kill), (c0, c1, bA, bB)


def kernel(L0_attn_x, L0_attn_Wq, L0_attn_Wk, L0_attn_Wv, L0_attn_Wo,
           L0_ffn_w_gate, L0_ffn_w_up, L0_ffn_w_down):
    x = np.asarray(L0_attn_x, dtype=np.float32)
    shared = _prep_shared(np.asarray(L0_attn_Wq, np.float32), np.asarray(L0_attn_Wk, np.float32),
                          np.asarray(L0_attn_Wv, np.float32), np.asarray(L0_attn_Wo, np.float32),
                          np.asarray(L0_ffn_w_gate, np.float32), np.asarray(L0_ffn_w_up, np.float32),
                          np.asarray(L0_ffn_w_down, np.float32))
    cosT, sinT = _rope_tables()
    if "nc" not in _CACHE:
        _CACHE["nc"] = build_nc(dump=None)[0]
    nc = _CACHE["nc"]
    in_maps, metas = [], []
    for k in range(NCORES):
        core, meta = _prep_core(k, x, cosT, sinT)
        m = dict(shared)
        m.update(core)
        in_maps.append(m)
        metas.append(meta)
    res = run_bass_kernel_spmd(nc, in_maps, core_ids=list(range(NCORES)))
    out = np.empty((B, T, D), dtype=np.float32)
    for k in range(NCORES):
        c0, c1, bA, bB = metas[k]
        o = np.asarray(res.results[k]["out"]).reshape(D, 1024).T
        out[bA, 512 * c0:512 * (c0 + 1)] = o[0:512]
        out[bB, 512 * c1:512 * (c1 + 1)] = o[512:1024]
    return out
